# version 03 (final)
# baseline (speedup 1.0000x reference)
import contextlib
import numpy as np
import ml_dtypes
import concourse.bass as bass
import concourse.mybir as mybir
from concourse.bass_utils import run_bass_kernel_spmd

F32 = mybir.dt.float32
BF16 = mybir.dt.bfloat16
AF = mybir.ActivationFunctionType
ALU = mybir.AluOpType
AX = mybir.AxisListType

ENGS = ("tensor", "vector", "scalar", "gpsimd", "sync")
NB = 2
S = 2048
D = 1024
H = 16
HD = 64
FD = 256
NT = NB * S
RND = 12582912.0


class B:
    def __init__(self, nc, es):
        self.nc = nc
        self.es = es
        self.sems = {}
        self.cnt = {}
        self.eng_sem = {e: self.mksem("S_" + e) for e in ENGS}
        self.dma_sem = {e: self.mksem("D_" + e) for e in ENGS}
        self.last_waited = {}
        self.thunks = {e: [] for e in ENGS}
        self._consts = {}

    def mksem(self, name):
        h = self.es.enter_context(self.nc.semaphore(name))
        self.sems[name] = h
        self.cnt[name] = 0
        return name

    def _waits_for(self, eng_name, deps):
        need = {}
        for d in deps:
            if d is None:
                continue
            if isinstance(d, list):
                for dd in d:
                    if dd is not None:
                        need[dd[0]] = max(need.get(dd[0], 0), dd[1])
                continue
            need[d[0]] = max(need.get(d[0], 0), d[1])
        waits = []
        for sem_name, val in sorted(need.items()):
            key = (eng_name, sem_name)
            if self.last_waited.get(key, 0) >= val:
                continue
            waits.append((sem_name, val))
            self.last_waited[key] = val
        return waits

    def emit(self, eng_name, fn, deps=(), inc=True):
        waits = self._waits_for(eng_name, deps)
        if inc:
            sem = self.eng_sem[eng_name]
            self.cnt[sem] += 1
            tok = (sem, self.cnt[sem])
        else:
            sem, tok = None, None

        def thunk(eng, waits=waits, fn=fn, sem=sem):
            for sn, v in waits:
                eng.wait_ge(self.sems[sn], v)
            instr = fn(eng)
            if sem is not None:
                instr.then_inc(self.sems[sem], 1)
        self.thunks[eng_name].append(thunk)
        return tok

    def dma(self, eng_name, out, in_, deps=(), group=None, **kw):
        waits = self._waits_for(eng_name, deps)
        if group is None:
            sem = self.dma_sem[eng_name]
        else:
            key = "G_" + group
            if key not in self.sems:
                self.mksem(key)
            sem = key
        self.cnt[sem] += 16
        tok = (sem, self.cnt[sem])

        def thunk(eng, waits=waits, out=out, in_=in_, sem=sem, kw=kw):
            for sn, v in waits:
                eng.wait_ge(self.sems[sn], v)
            eng.dma_start(out=out, in_=in_, **kw).then_inc(self.sems[sem], 16)
        self.thunks[eng_name].append(thunk)
        return tok

    def finalize(self):
        with self.nc.Block() as block:
            for e in ENGS:
                ths = self.thunks[e]

                def section(eng, ths=ths):
                    for t in ths:
                        t(eng)
                getattr(block, e)(section)

    def tok(self, eng):
        sem = self.eng_sem[eng]
        return (sem, self.cnt[sem])

    def bar(self):
        return [self.tok(e) for e in ("tensor", "vector", "scalar")]

    def constcol(self, value, dtype=F32, rows=128):
        key = (float(value), dtype, rows)
        if key not in self._consts:
            t = self.es.enter_context(
                self.nc.sbuf_tensor(f"cst{len(self._consts)}", [rows, 1], dtype))
            tok = self.emit("vector", lambda e: e.memset(t[:, 0:1], value))
            self._consts[key] = (t, tok)
        t, tok = self._consts[key]
        return t[:, 0:1], tok

    def constrow(self, value, cols, dtype=F32, name=None):
        key = (float(value), dtype, "row", cols)
        if key not in self._consts:
            t = self.es.enter_context(
                self.nc.sbuf_tensor(f"csr{len(self._consts)}", [1, cols], dtype))
            tok = self.emit("vector", lambda e: e.memset(t[0:1, :], value))
            self._consts[key] = (t, tok)
        t, tok = self._consts[key]
        return t[0:1, :], tok


def _ln_psum(b, parts, tmp, out_ap, deps, eps=1e-5):
    sq, r4 = tmp["sq"], tmp["r4"]
    war_out = tmp.get("war_ln_out")
    toks = []
    t_rs = []
    off = 0
    col0 = 0
    for ap, w in parts:
        t1 = b.emit("vector", lambda e, ap=ap, off=off: e.tensor_reduce(
            out=sq[:, off:off + 1], in_=ap, axis=AX.X, op=ALU.add), deps=list(deps) + [war_out])
        t2 = b.emit("vector", lambda e, ap=ap, w=w, col0=col0: e.tensor_copy(
            out=out_ap[:, col0:col0 + w], in_=ap), deps=list(deps) + [war_out, t1])
        t2b = b.emit("vector", lambda e, w=w, col0=col0: e.tensor_tensor(
            out=out_ap[:, col0:col0 + w], in0=out_ap[:, col0:col0 + w],
            in1=out_ap[:, col0:col0 + w], op=ALU.mult), deps=[t2])
        t2r = b.emit("vector", lambda e, off=off, w=w, col0=col0: e.tensor_reduce(
            out=sq[:, off + 2:off + 3], in_=out_ap[:, col0:col0 + w], axis=AX.X, op=ALU.add),
            deps=[t2b])
        t_rs += [t1, t2, t2b, t2r]
        off += 1
        col0 += w
    n = float(sum(w for _, w in parts))
    t3 = b.emit("vector", lambda e: e.tensor_tensor(
        out=r4[:, 0:1], in0=sq[:, 0:1], in1=sq[:, 1:2], op=ALU.add), deps=t_rs)
    t4 = b.emit("vector", lambda e: e.tensor_tensor(
        out=r4[:, 1:2], in0=sq[:, 2:3], in1=sq[:, 3:4], op=ALU.add), deps=t_rs)
    t5 = b.emit("vector", lambda e: e.tensor_scalar(
        out=r4[:, 0:1], in0=r4[:, 0:1], scalar1=1.0 / n, scalar2=None, op0=ALU.mult), deps=[t3])
    t6 = b.emit("vector", lambda e: e.tensor_scalar(
        out=r4[:, 1:2], in0=r4[:, 1:2], scalar1=1.0 / n, scalar2=None, op0=ALU.mult), deps=[t4])
    t7 = b.emit("vector", lambda e: e.tensor_tensor(
        out=r4[:, 2:3], in0=r4[:, 0:1], in1=r4[:, 0:1], op=ALU.mult), deps=[t5])
    t8 = b.emit("vector", lambda e: e.tensor_tensor(
        out=r4[:, 2:3], in0=r4[:, 1:2], in1=r4[:, 2:3], op=ALU.subtract), deps=[t6, t7])
    t8b = b.emit("vector", lambda e: e.tensor_scalar(
        out=r4[:, 2:3], in0=r4[:, 2:3], scalar1=eps, scalar2=None, op0=ALU.add), deps=[t8])
    t9 = b.emit("scalar", lambda e: e.activation(
        out=r4[:, 3:4], in_=r4[:, 2:3], func=AF.Sqrt), deps=[t8b])
    t10 = b.emit("vector", lambda e: e.reciprocal(out=r4[:, 3:4], in_=r4[:, 3:4]), deps=[t9])
    off = 0
    col = 0
    for ap, w in parts:
        tt = b.emit("vector", lambda e, ap=ap, col=col, w=w: e.tensor_scalar(
            out=out_ap[:, col:col + w], in0=ap, scalar1=r4[:, 0:1], scalar2=r4[:, 3:4],
            op0=ALU.subtract, op1=ALU.mult), deps=[t5, t10, war_out] + t_rs)
        toks.append(tt)
        col += w
    return toks


def _lnT_from_x(b, x_hbm, row0, dstT, dst_col0, tmps, identf, t_identf, psA, extra_dep=None):
    xt, xn, sq, r4 = tmps["xt"], tmps["xn"], tmps["sq"], tmps["r4"]
    stride = tmps["strideT"]
    deps = [extra_dep] if extra_dep else []
    t_l = b.dma("sync", xt[:, 0:D], x_hbm[row0:row0 + 128, :], deps=deps + [tmps.get("war_xt")], group="xt")
    tmps["war_ln_out"] = tmps.get("war_xn")
    toks_ln = _ln_psum(b, [(xt[:, 0:512], 512), (xt[:, 512:1024], 512)], tmps, xn, [t_l])
    tmps["war_ln_out"] = None
    outs = []
    t_tr = None
    for j in range(8):
        pt, war = psA.get("tr")
        t_tr = b.emit("tensor", lambda e, j=j, pt=pt: e.transpose(
            pt[:, 0:128], xn[:, j * 128:(j + 1) * 128], identf), deps=toks_ln + [t_identf, war])
        t_cp = b.emit("scalar", lambda e, j=j, pt=pt: e.copy(
            out=dstT[:, j * stride + dst_col0: j * stride + dst_col0 + 128], in_=pt[:, 0:128]),
            deps=[t_tr, extra_dep]),
        psA.done("tr", t_cp[0])
        outs.append(t_cp[0])
    tmps["war_xt"] = toks_ln
    tmps["war_xn"] = t_tr
    return outs


class PsumPool:

    def __init__(self, nc, es, specs):
        self.t = {}
        self.last = {}
        self.i = {}
        self.j = {}
        for tag, (shape, dt, nb) in specs.items():
            self.t[tag] = [es.enter_context(nc.psum_tensor(f"ps_{tag}{k}", shape, dt))
                           for k in range(nb)]
            self.last[tag] = [None] * nb
            self.i[tag] = 0
            self.j[tag] = 0

    def get(self, tag):
        k = self.i[tag] % len(self.t[tag])
        self.i[tag] += 1
        return self.t[tag][k], self.last[tag][k]

    def done(self, tag, token):
        k = self.j[tag] % len(self.t[tag])
        self.j[tag] += 1
        assert self.j[tag] <= self.i[tag], f"done before get on {tag}"
        self.last[tag][k] = token


def _load_wT_bf(b, w_hbm, dst_bf, rows, cols, tmpw, group, deps=()):
    toks = []
    war = None
    nh = max(1, cols // 2048)
    cw = cols // nh
    for j in range(rows // 128):
        for hh in range(nh):
            t_l = b.dma("sync", tmpw[:, 0:cw], w_hbm[j * 128:(j + 1) * 128, hh * cw:(hh + 1) * cw],
                        deps=list(deps) + [war], group=group)
            t_c = b.emit("scalar", lambda e, j=j, hh=hh, cw=cw: e.copy(
                out=dst_bf[:, j * cols + hh * cw:j * cols + (hh + 1) * cw], in_=tmpw[:, 0:cw]),
                deps=[t_l])
            war = t_c
            toks.append(t_c)
    return toks


def _bias_cols(b, bias_hbm, dst, n, group):
    toks = []
    for j in range(n):
        toks.append(b.dma("sync", dst[:, j:j + 1],
                          bias_hbm[j * 128:(j + 1) * 128].rearrange("(p one) -> p one", one=1),
                          group=group))
    return toks


def _post_ln_residual(b, p0, p1, t_mm, x_hbm, out_hbm, row0, tmps, extra=()):
    xr, xn2 = tmps["xr"], tmps["xn2"]
    t_x = b.dma("sync", xr[:, 0:D], x_hbm[row0:row0 + 128, :],
                deps=[tmps.get("war_xr")], group="xr")
    tmps["war_ln_out"] = tmps.get("war_xn2")
    toks = _ln_psum(b, [(p0[:, 0:512], 512), (p1[:, 0:512], 512)], tmps, xn2,
                    list(t_mm) + list(extra))
    tmps["war_ln_out"] = None
    t_a = b.emit("vector", lambda e: e.tensor_tensor(
        out=xn2[:, 0:D], in0=xn2[:, 0:D], in1=xr[:, 0:D], op=ALU.add), deps=toks + [t_x])
    tmps["war_xr"] = t_a
    t_o = b.dma("gpsimd", out_hbm[row0:row0 + 128, :], xn2[:, 0:D], deps=[t_a], group="xout")
    tmps["war_xn2"] = t_o
    return t_o, t_a


def build_p1():
    nc = bass.Bass("TRN2", target_bir_lowering=False)
    x = nc.dram_tensor("x", [NT, D], F32, kind="ExternalInput").ap()
    wqT = nc.dram_tensor("wqT", [D, D], F32, kind="ExternalInput").ap()
    wkT = nc.dram_tensor("wkT", [D, D], F32, kind="ExternalInput").ap()
    wvT = nc.dram_tensor("wvT", [D, D], F32, kind="ExternalInput").ap()
    woT = nc.dram_tensor("woT", [D, D], F32, kind="ExternalInput").ap()
    wfT = nc.dram_tensor("wfT", [128, FD], F32, kind="ExternalInput").ap()
    identf = nc.dram_tensor("identf", [128, 128], F32, kind="ExternalInput").ap()
    identb = nc.dram_tensor("identb", [128, 128], BF16, kind="ExternalInput").ap()
    x1 = nc.dram_tensor("x1", [NT, D], F32, kind="ExternalOutput").ap()

    with contextlib.ExitStack() as es:
        E = es.enter_context
        b = B(nc, es)
        AR1 = E(nc.sbuf_tensor("AR1", [128, 8 * S], BF16))
        AR2 = E(nc.sbuf_tensor("AR2", [128, 8 * S], BF16))
        AR3 = E(nc.sbuf_tensor("AR3", [128, 8 * S], BF16))
        AR4 = E(nc.sbuf_tensor("AR4", [128, 8 * S], BF16))
        AW = E(nc.sbuf_tensor("AW", [128, 8 * D], BF16))
        qfT = E(nc.sbuf_tensor("qfT", [128, 2 * S], BF16))
        kf = E(nc.sbuf_tensor("kf", [128, 16 * FD], BF16))
        kvb = E(nc.sbuf_tensor("kvb", [128, 130], BF16))
        wfb = E(nc.sbuf_tensor("wfb", [128, FD], BF16))
        sif = E(nc.sbuf_tensor("sif", [128, 128], F32))
        sib = E(nc.sbuf_tensor("sib", [128, 128], BF16))
        xt = E(nc.sbuf_tensor("xt", [128, D], F32))
        xn = E(nc.sbuf_tensor("xn", [128, D], F32))
        xr = E(nc.sbuf_tensor("xr", [128, D], F32))
        xn2 = E(nc.sbuf_tensor("xn2", [128, D], F32))
        sq = E(nc.sbuf_tensor("sq", [128, 8], F32))
        r4 = E(nc.sbuf_tensor("r4", [128, 4], F32))
        tw = E(nc.sbuf_tensor("tw", [128, D], F32))
        twf = E(nc.sbuf_tensor("twf", [128, FD], F32))
        e1 = E(nc.sbuf_tensor("e1", [128, 512], F32))
        e2 = E(nc.sbuf_tensor("e2", [128, 512], F32))
        e3 = E(nc.sbuf_tensor("e3", [128, 512], F32))
        rno = E(nc.sbuf_tensor("rno", [128, 1], F32))
        ps = PsumPool(nc, es, {
            "mm": ([128, 512], F32, 3),
            "tr": ([128, 128], F32, 1),
            "trb": ([128, 128], BF16, 1),
            "kvv": ([128, 128], F32, 1),
            "kvs": ([128, 2], F32, 1),
            "oh": ([128, 65], F32, 1),
        })
        tmps = {"xt": xt, "xn": xn, "xr": xr, "xn2": xn2, "sq": sq, "r4": r4,
                "strideT": S}
        t_if = b.dma("sync", sif[:], identf[:, :], group="ident")
        t_ib = b.dma("sync", sib[:], identb[:, :], group="identb")
        t_wfl = b.dma("sync", twf[0:128, :], wfT[:, :], group="wf")
        t_wfc = b.emit("scalar", lambda e: e.copy(out=wfb[0:128, :], in_=twf[0:128, :]), deps=[t_wfl])
        onesb, t_onesb = b.constcol(1.0, BF16)

        for bb in range(NB):
            base = bb * S
            phA = []
            bar0 = b.bar()
            for st in range(16):
                phA += _lnT_from_x(b, x, base + st * 128, AR1, st * 128, tmps, sif[:], t_if, ps,
                                   extra_dep=bar0)
            if P1_STOP < "B":
                continue
            phB = []
            for (wt, dst, mode) in ((wqT, AR2, "f"), (wkT, AR3, "f"), (wvT, AR4, "t")):
                t_w = _load_wT_bf(b, wt, AW, D, D, tw, "w", deps=(phB[-1:] or []) + [b.tok("tensor")])
                if mode == "f":
                    for m in range(8):
                        for n in range(4):
                            pm, war = ps.get("mm")
                            tk = None
                            for k in range(8):
                                tk = b.emit("tensor", lambda e, pm=pm, m=m, n=n, k=k: e.matmul(
                                    pm[:, 0:512],
                                    AW[:, k * D + m * 128:k * D + (m + 1) * 128],
                                    AR1[:, k * S + n * 512:k * S + (n + 1) * 512],
                                    start=(k == 0), stop=(k == 7)),
                                    deps=(phA + t_w + [war]) if k == 0 else [], inc=(k == 7))
                            t_e = b.emit("scalar", lambda e, pm=pm, m=m, n=n, dst=dst: e.copy(
                                out=dst[:, m * S + n * 512:m * S + (n + 1) * 512], in_=pm[:, 0:512]),
                                deps=[tk])
                            ps.done("mm", t_e)
                            phB.append(t_e)
                else:
                    for sm in range(16):
                        for n in range(2):
                            pm, war = ps.get("mm")
                            tk = None
                            for k in range(8):
                                tk = b.emit("tensor", lambda e, pm=pm, sm=sm, n=n, k=k: e.matmul(
                                    pm[:, 0:512],
                                    AR1[:, k * S + sm * 128:k * S + (sm + 1) * 128],
                                    AW[:, k * D + n * 512:k * D + (n + 1) * 512],
                                    start=(k == 0), stop=(k == 7)),
                                    deps=(phA + t_w + [war]) if k == 0 else [], inc=(k == 7))
                            t_e = b.emit("scalar", lambda e, pm=pm, sm=sm, n=n: e.copy(
                                out=AR4[:, sm * D + n * 512:sm * D + (n + 1) * 512], in_=pm[:, 0:512]),
                                deps=[tk])
                            ps.done("mm", t_e)
                            phB.append(t_e)
            if P1_STOP < "C":
                continue
            phC = []
            barB = b.tok("tensor")
            prev_elu_s, prev_elu_v = None, None
            for h in range(16):
                koff = (h % 2) * 64
                kch = h // 2
                barB = b.tok("tensor")
                t_qf = []
                for m2 in range(2):
                    for n in range(4):
                        pm, war = ps.get("mm")
                        tk = b.emit("tensor", lambda e, pm=pm, m2=m2, n=n, koff=koff, kch=kch: e.matmul(
                            pm[:, 0:512],
                            wfb[koff:koff + 64, m2 * 128:(m2 + 1) * 128],
                            AR2[koff:koff + 64, kch * S + n * 512:kch * S + (n + 1) * 512],
                            start=True, stop=True),
                            deps=phB + [t_wfc, war, t_onesb])
                        ta = b.emit("vector", lambda e, pm=pm: e.tensor_scalar(
                            out=e1[:, 0:512], in0=pm[:, 0:512], scalar1=0.0, scalar2=None,
                            op0=ALU.min), deps=[tk, prev_elu_s])
                        tb = b.emit("scalar", lambda e: e.activation(
                            out=e2[:, 0:512], in_=e1[:, 0:512], func=AF.Exp), deps=[ta, prev_elu_v])
                        tc = b.emit("scalar", lambda e, pm=pm: e.activation(
                            out=e3[:, 0:512], in_=pm[:, 0:512], func=AF.Relu), deps=[tk, tb, prev_elu_v])
                        td = b.emit("vector", lambda e, m2=m2, n=n: e.tensor_tensor(
                            out=qfT[:, m2 * S + n * 512:m2 * S + (n + 1) * 512],
                            in0=e2[:, 0:512], in1=e3[:, 0:512], op=ALU.add), deps=[tb, tc, barB])
                        prev_elu_s, prev_elu_v = tc, td
                        ps.done("mm", td)
                        t_qf.append(td)
                t_kf = []
                for sm in range(16):
                    pm, war = ps.get("mm")
                    tk = b.emit("tensor", lambda e, pm=pm, sm=sm, koff=koff, kch=kch: e.matmul(
                        pm[:, 0:256],
                        AR3[koff:koff + 64, kch * S + sm * 128:kch * S + (sm + 1) * 128],
                        wfb[koff:koff + 64, 0:256], start=True, stop=True),
                        deps=phB + [t_wfc, war])
                    ta = b.emit("vector", lambda e, pm=pm: e.tensor_scalar(
                        out=e1[:, 0:256], in0=pm[:, 0:256], scalar1=0.0, scalar2=None,
                        op0=ALU.min), deps=[tk, prev_elu_s])
                    tb = b.emit("scalar", lambda e: e.activation(
                        out=e2[:, 0:256], in_=e1[:, 0:256], func=AF.Exp), deps=[ta, prev_elu_v])
                    tc = b.emit("scalar", lambda e, pm=pm: e.activation(
                        out=e3[:, 0:256], in_=pm[:, 0:256], func=AF.Relu), deps=[tk, tb, prev_elu_v])
                    td = b.emit("vector", lambda e, sm=sm: e.tensor_tensor(
                        out=kf[:, sm * FD:(sm + 1) * FD],
                        in0=e2[:, 0:256], in1=e3[:, 0:256], op=ALU.add), deps=[tb, tc, barB])
                    prev_elu_s, prev_elu_v = tc, td
                    ps.done("mm", td)
                    t_kf.append(td)
                pkv, war_kv = ps.get("kvv")
                pks, war_ks = ps.get("kvs")
                t_kv_mm = []
                for f2 in range(2):
                    tk = None
                    tks = None
                    for sm in range(16):
                        tk = b.emit("tensor", lambda e, f2=f2, sm=sm, pkv=pkv, h=h: e.matmul(
                            pkv[:, f2 * 64:f2 * 64 + 64],
                            kf[:, sm * FD + f2 * 128:sm * FD + f2 * 128 + 128],
                            AR4[:, sm * D + h * 64:sm * D + (h + 1) * 64],
                            start=(sm == 0), stop=(sm == 15)),
                            deps=(t_kf + [war_kv]) if sm == 0 else [], inc=(sm == 15))
                        tks = b.emit("tensor", lambda e, f2=f2, sm=sm, pks=pks: e.matmul(
                            pks[:, f2:f2 + 1],
                            kf[:, sm * FD + f2 * 128:sm * FD + f2 * 128 + 128],
                            onesb,
                            start=(sm == 0), stop=(sm == 15)),
                            deps=[t_onesb, war_ks] if sm == 0 else [], inc=(sm == 15))
                    t_kv_mm += [tk, tks]
                t_kvc = b.emit("scalar", lambda e, pkv=pkv: e.copy(
                    out=kvb[:, 0:64], in_=pkv[:, 0:64]), deps=t_kv_mm)
                t_kvc2 = b.emit("scalar", lambda e, pkv=pkv: e.copy(
                    out=kvb[:, 65:129], in_=pkv[:, 64:128]), deps=t_kv_mm + [t_kvc])
                t_kvc3 = b.emit("scalar", lambda e, pks=pks: e.copy(
                    out=kvb[:, 64:65], in_=pks[:, 0:1]), deps=t_kv_mm + [t_kvc2])
                t_kvc4 = b.emit("scalar", lambda e, pks=pks: e.copy(
                    out=kvb[:, 129:130], in_=pks[:, 1:2]), deps=t_kv_mm + [t_kvc3])
                t_kvc = t_kvc4
                ps.done("kvv", t_kvc)
                ps.done("kvs", t_kvc)
                for sm in range(16):
                    po, war_o = ps.get("oh")
                    tk = None
                    for f2 in range(2):
                        tk = b.emit("tensor", lambda e, po=po, sm=sm, f2=f2: e.matmul(
                            po[:, 0:65],
                            qfT[:, f2 * S + sm * 128:f2 * S + (sm + 1) * 128],
                            kvb[:, f2 * 65:(f2 + 1) * 65],
                            start=(f2 == 0), stop=(f2 == 1)),
                            deps=(t_qf + [t_kvc, war_o]) if f2 == 0 else [], inc=(f2 == 1))
                    ta = b.emit("vector", lambda e, po=po: e.tensor_scalar(
                        out=rno[:, 0:1], in0=po[:, 64:65], scalar1=1e-6, scalar2=None,
                        op0=ALU.add), deps=[tk])
                    tb = b.emit("vector", lambda e: e.reciprocal(out=rno[:, 0:1], in_=rno[:, 0:1]),
                                deps=[ta])
                    tc = b.emit("vector", lambda e, po=po, sm=sm, h=h: e.tensor_scalar(
                        out=AR1[:, sm * D + h * 64:sm * D + (h + 1) * 64],
                        in0=po[:, 0:64], scalar1=rno[:, 0:1], scalar2=None, op0=ALU.mult),
                        deps=[tb, barB])
                    ps.done("oh", tc)
                    phC.append(tc)
            if P1_STOP < "D":
                continue
            phD = []
            barC = b.tok("tensor")
            for st in range(16):
                for j in range(8):
                    pt, war = ps.get("trb")
                    t_tr = b.emit("tensor", lambda e, st=st, j=j, pt=pt: e.transpose(
                        pt[:, 0:128], AR1[:, st * D + j * 128:st * D + (j + 1) * 128], sib[:]),
                        deps=phC + [t_ib, war])
                    t_cp = b.emit("scalar", lambda e, st=st, j=j, pt=pt: e.copy(
                        out=AR3[:, j * S + st * 128:j * S + (st + 1) * 128], in_=pt[:, 0:128]),
                        deps=[t_tr, barC])
                    ps.done("trb", t_cp)
                    phD.append(t_cp)
            if P1_STOP < "E":
                continue
            t_wo = _load_wT_bf(b, woT, AW, D, D, tw, "w", deps=phB + [b.tok("tensor")])
            for st in range(16):
                pms = []
                mmt = []
                for n in range(2):
                    pm, war = ps.get("mm")
                    tk = None
                    for k in range(8):
                        tk = b.emit("tensor", lambda e, pm=pm, st=st, n=n, k=k: e.matmul(
                            pm[:, 0:512],
                            AR3[:, k * S + st * 128:k * S + (st + 1) * 128],
                            AW[:, k * D + n * 512:k * D + (n + 1) * 512],
                            start=(k == 0), stop=(k == 7)),
                            deps=(phD + t_wo + [war]) if k == 0 else [], inc=(k == 7))
                    pms.append(pm)
                    mmt.append(tk)
                t_o, t_a = _post_ln_residual(b, pms[0], pms[1], mmt, x, x1, base + st * 128, tmps)
                ps.done("mm", t_a)
                ps.done("mm", t_a)
        b.finalize()
    return nc


def build_p2():
    nc = bass.Bass("TRN2", target_bir_lowering=False)
    x = nc.dram_tensor("x", [NT, D], F32, kind="ExternalInput").ap()
    w1T = nc.dram_tensor("w1T", [D, 4 * D], F32, kind="ExternalInput").ap()
    b1 = nc.dram_tensor("b1", [4 * D], F32, kind="ExternalInput").ap()
    w2T = nc.dram_tensor("w2T", [4 * D, D], F32, kind="ExternalInput").ap()
    b2 = nc.dram_tensor("b2", [D], F32, kind="ExternalInput").ap()
    identf = nc.dram_tensor("identf", [128, 128], F32, kind="ExternalInput").ap()
    x2 = nc.dram_tensor("x2", [NT, D], F32, kind="ExternalOutput").ap()
    FF = 4 * D

    with contextlib.ExitStack() as es:
        E = es.enter_context
        b = B(nc, es)
        w1q = E(nc.sbuf_tensor("w1q", [128, 8 * FF], BF16))
        w2q = E(nc.sbuf_tensor("w2q", [128, 32 * D], BF16))
        hT = E(nc.sbuf_tensor("hT", [128, 32 * 512], BF16))
        lnT = E(nc.sbuf_tensor("lnT", [128, 8 * 512], BF16))
        sif = E(nc.sbuf_tensor("sif", [128, 128], F32))
        tw = E(nc.sbuf_tensor("tw", [128, 2048], F32))
        b1c = E(nc.sbuf_tensor("b1c", [128, 32], F32))
        b2r = E(nc.sbuf_tensor("b2r", [1, D], BF16))
        tb2 = E(nc.sbuf_tensor("tb2", [1, D], F32))
        o1r = E(nc.sbuf_tensor("o1r", [1, 128], BF16))
        xt = E(nc.sbuf_tensor("xt", [128, D], F32))
        xn = E(nc.sbuf_tensor("xn", [128, D], F32))
        xr = E(nc.sbuf_tensor("xr", [128, D], F32))
        xn2 = E(nc.sbuf_tensor("xn2", [128, D], F32))
        sq = E(nc.sbuf_tensor("sq", [128, 8], F32))
        r4 = E(nc.sbuf_tensor("r4", [128, 4], F32))
        gac = E(nc.sbuf_tensor("gac", [128, 4], F32))
        grow = E(nc.sbuf_tensor("grow", [1, 128], F32))
        gsc = E(nc.sbuf_tensor("gsc", [1, 4], F32))
        go = E(nc.sbuf_tensor("go", [1, 128], F32))
        ginv = E(nc.sbuf_tensor("ginv", [128, 2], F32))
        ps = PsumPool(nc, es, {
            "mm": ([128, 512], F32, 4),
            "tr": ([128, 128], F32, 2),
            "b": ([128, 1], F32, 2),
        })
        tmps = {"xt": xt, "xn": xn, "xr": xr, "xn2": xn2, "sq": sq, "r4": r4,
                "strideT": 512}
        t_if = b.dma("sync", sif[:], identf[:, :], group="ident")
        t_zg = b.emit("vector", lambda e: e.memset(gac[:, 0:4], 0.0))
        t_go = b.emit("vector", lambda e: e.memset(go[0:1, :], 1.0))

        t_gm = [t_zg]
        war_w = None
        for (wt, ntile, wid, col) in ((w1T, 8, FF, 0), (w2T, 32, D, 1)):
            nh = max(1, wid // 2048)
            cw = wid // nh
            for j in range(ntile):
                for hh in range(nh):
                    t_l = b.dma("sync", tw[:, 0:cw], wt[j * 128:(j + 1) * 128, hh * cw:(hh + 1) * cw],
                                deps=[war_w], group="wg")
                    t_r = b.emit("vector", lambda e, cw=cw: e.tensor_reduce(
                        out=r4[:, 0:1], in_=tw[:, 0:cw], axis=AX.X, op=ALU.add,
                        apply_absolute_value=True), deps=[t_l, t_gm[-1]])
                    t_acc = b.emit("vector", lambda e, col=col: e.tensor_tensor(
                        out=gac[:, col:col + 1], in0=gac[:, col:col + 1], in1=r4[:, 0:1],
                        op=ALU.add), deps=[t_r])
                    war_w = t_acc
                    t_gm.append(t_acc)
        t_r1 = b.dma("gpsimd", grow[0:1, 0:128], gac[0:128, 0:1], deps=[t_gm[-1]], group="g1")
        t_s1 = b.emit("vector", lambda e: e.tensor_reduce(
            out=gsc[0:1, 0:1], in_=grow[0:1, 0:128], axis=AX.X, op=ALU.add), deps=[t_r1])
        t_r2 = b.dma("gpsimd", grow[0:1, 0:128], gac[0:128, 1:2], deps=[t_gm[-1], t_s1], group="g1")
        t_s2 = b.emit("vector", lambda e: e.tensor_reduce(
            out=gsc[0:1, 1:2], in_=grow[0:1, 0:128], axis=AX.X, op=ALU.add), deps=[t_r2])
        t_i = []
        for col, N in ((0, D * FF), (1, D * FF)):
            ta = b.emit("vector", lambda e, col=col, N=N: e.tensor_scalar(
                out=gsc[0:1, col:col + 1], in0=gsc[0:1, col:col + 1],
                scalar1=1.0 / N, scalar2=1e-5, op0=ALU.mult, op1=ALU.add),
                deps=[t_s1, t_s2])
            tb = b.emit("vector", lambda e, col=col: e.reciprocal(
                out=gsc[0:1, 2 + col:3 + col], in_=gsc[0:1, col:col + 1]), deps=[ta])
            t_i.append(tb)
        t_bc = []
        for col in range(2):
            pb_, war = ps.get("b")
            tm = b.emit("tensor", lambda e, col=col, pb_=pb_: e.matmul(
                pb_[:, 0:1], go[0:1, 0:128], gsc[0:1, 2 + col:3 + col],
                start=True, stop=True), deps=t_i + [t_go, war])
            tc = b.emit("vector", lambda e, col=col, pb_=pb_: e.tensor_copy(
                out=ginv[:, col:col + 1], in_=pb_[:, 0:1]), deps=[tm])
            ps.done("b", tc)
            t_bc.append(tc)
        t_qz = []
        war_w2 = war_w
        for (wt, ntile, wid, col, dst) in ((w1T, 8, FF, 0, w1q), (w2T, 32, D, 1, w2q)):
            nh = max(1, wid // 2048)
            cw = wid // nh
            for j in range(ntile):
                for hh in range(nh):
                    t_l = b.dma("sync", tw[:, 0:cw], wt[j * 128:(j + 1) * 128, hh * cw:(hh + 1) * cw],
                                deps=[war_w2], group="wg")
                    ta = b.emit("vector", lambda e, cw=cw, col=col: e.tensor_scalar(
                        out=tw[:, 0:cw], in0=tw[:, 0:cw], scalar1=ginv[:, col:col + 1],
                        scalar2=RND, op0=ALU.mult, op1=ALU.add), deps=[t_l, t_bc[col]])
                    tb = b.emit("vector", lambda e, cw=cw: e.tensor_scalar(
                        out=tw[:, 0:cw], in0=tw[:, 0:cw], scalar1=RND, scalar2=None,
                        op0=ALU.subtract), deps=[ta])
                    tc = b.emit("vector", lambda e, cw=cw, j=j, hh=hh, wid=wid, dst=dst: e.tensor_scalar(
                        out=dst[:, j * wid + hh * cw:j * wid + (hh + 1) * cw], in0=tw[:, 0:cw],
                        scalar1=1.0, scalar2=-1.0, op0=ALU.min, op1=ALU.max), deps=[tb])
                    war_w2 = tc
                    t_qz.append(tc)
        t_b1 = _bias_cols(b, b1, b1c, 32, "b1")
        t_b2l = b.dma("sync", tb2[0:1, 0:D], b2[:].rearrange("(one d) -> one d", one=1), group="b2")
        t_b2c = b.emit("scalar", lambda e: e.copy(out=b2r[0:1, :], in_=tb2[0:1, :]), deps=[t_b2l])
        t_o1 = b.emit("vector", lambda e: e.memset(o1r[0:1, :], 1.0))

        for sb in range(8):
            phL = []
            barS = b.bar()
            for st in range(4):
                phL += _lnT_from_x(b, x, sb * 512 + st * 128, lnT, st * 128, tmps,
                                   sif[:], t_if, ps, extra_dep=barS)
            phH = []
            for fc in range(32):
                pm, war = ps.get("mm")
                tk = None
                for k in range(8):
                    tk = b.emit("tensor", lambda e, pm=pm, fc=fc, k=k: e.matmul(
                        pm[:, 0:512],
                        w1q[:, k * FF + fc * 128:k * FF + (fc + 1) * 128],
                        lnT[:, k * 512:(k + 1) * 512],
                        start=(k == 0), stop=(k == 7)),
                        deps=(phL + t_qz + [war]) if k == 0 else [], inc=(k == 7))
                t_h = b.emit("scalar", lambda e, pm=pm, fc=fc: e.activation(
                    out=hT[:, fc * 512:(fc + 1) * 512], in_=pm[:, 0:512], func=AF.Relu,
                    bias=b1c[:, fc:fc + 1], scale=1.0), deps=[tk, barS] + t_b1)
                ps.done("mm", t_h)
                phH.append(t_h)
            for ss in range(4):
                pms, mmt = [], []
                for n in range(2):
                    pm, war = ps.get("mm")
                    tk = None
                    for fc in range(32):
                        tk = b.emit("tensor", lambda e, pm=pm, ss=ss, n=n, fc=fc: e.matmul(
                            pm[:, 0:512],
                            hT[:, fc * 512 + ss * 128:fc * 512 + (ss + 1) * 128],
                            w2q[:, fc * D + n * 512:fc * D + (n + 1) * 512],
                            start=(fc == 0), stop=False),
                            deps=(phH + [war]) if fc == 0 else [], inc=False)
                    tk = b.emit("tensor", lambda e, pm=pm, n=n: e.matmul(
                        pm[:, 0:512], o1r[0:1, 0:128], b2r[0:1, n * 512:(n + 1) * 512],
                        start=False, stop=True), deps=[t_o1, t_b2c])
                    pms.append(pm)
                    mmt.append(tk)
                t_o, t_a = _post_ln_residual(b, pms[0], pms[1], mmt, x, x2,
                                             sb * 512 + ss * 128, tmps)
                ps.done("mm", t_a)
                ps.done("mm", t_a)
        b.finalize()
    return nc


def build_p3():
    nc = bass.Bass("TRN2", target_bir_lowering=False)
    x = nc.dram_tensor("x", [NT, D], F32, kind="ExternalInput").ap()
    wqT = nc.dram_tensor("wqT", [D, D], F32, kind="ExternalInput").ap()
    wkT = nc.dram_tensor("wkT", [D, D], F32, kind="ExternalInput").ap()
    wvT = nc.dram_tensor("wvT", [D, D], F32, kind="ExternalInput").ap()
    woT = nc.dram_tensor("woT", [D, D], F32, kind="ExternalInput").ap()
    bq = nc.dram_tensor("bq", [D], F32, kind="ExternalInput").ap()
    bk = nc.dram_tensor("bk", [D], F32, kind="ExternalInput").ap()
    bv = nc.dram_tensor("bv", [D], F32, kind="ExternalInput").ap()
    bo = nc.dram_tensor("bo", [D], F32, kind="ExternalInput").ap()
    gwT = nc.dram_tensor("gwT", [D, 8], F32, kind="ExternalInput").ap()
    identf = nc.dram_tensor("identf", [128, 128], F32, kind="ExternalInput").ap()
    x3 = nc.dram_tensor("x3", [NT, D], F32, kind="ExternalOutput").ap()
    ln3T = nc.dram_tensor("ln3T", [D, NT], BF16, kind="ExternalOutput").ap()
    logits = nc.dram_tensor("logits", [NT, 8], F32, kind="ExternalOutput").ap()

    with contextlib.ExitStack() as es:
        E = es.enter_context
        b = B(nc, es)
        AR1 = E(nc.sbuf_tensor("AR1", [128, 8 * S], BF16))
        AR2 = E(nc.sbuf_tensor("AR2", [128, 8 * S], BF16))
        AR3 = E(nc.sbuf_tensor("AR3", [128, 8 * S], BF16))
        AR4 = E(nc.sbuf_tensor("AR4", [128, 8 * S], BF16))
        AW = E(nc.sbuf_tensor("AW", [128, 8 * D], BF16))
        expT = E(nc.sbuf_tensor("expT", [128, 16 * 512], BF16))
        sif = E(nc.sbuf_tensor("sif", [128, 128], F32))
        xt = E(nc.sbuf_tensor("xt", [128, D], F32))
        xn = E(nc.sbuf_tensor("xn", [128, D], F32))
        xr = E(nc.sbuf_tensor("xr", [128, D], F32))
        xn2 = E(nc.sbuf_tensor("xn2", [128, D], F32))
        sq = E(nc.sbuf_tensor("sq", [128, 8], F32))
        r4 = E(nc.sbuf_tensor("r4", [128, 4], F32))
        tw = E(nc.sbuf_tensor("tw", [128, D], F32))
        bqc = E(nc.sbuf_tensor("bqc", [128, 8], F32))
        bkc = E(nc.sbuf_tensor("bkc", [128, 8], F32))
        brow = E(nc.sbuf_tensor("brow", [1, D], BF16))
        brow2 = E(nc.sbuf_tensor("brow2", [1, D], BF16))
        trow = E(nc.sbuf_tensor("trow", [1, D], F32))
        o1r = E(nc.sbuf_tensor("o1r", [1, 128], BF16))
        o1c = E(nc.sbuf_tensor("o1c", [128, 1], BF16))
        o64 = E(nc.sbuf_tensor("o64", [1, 64], F32))
        rrow = E(nc.sbuf_tensor("rrow", [1, 512], F32))
        rb = E(nc.sbuf_tensor("rb", [64, 512], F32))
        gwb = E(nc.sbuf_tensor("gwb", [128, 8 * 8], BF16))
        tg = E(nc.sbuf_tensor("tg", [128, 8], F32))
        lgt = E(nc.sbuf_tensor("lgt", [128, 32 * 8], F32))
        ln3s = E(nc.sbuf_tensor("ln3s", [128, 8 * 128], BF16))
        ps = PsumPool(nc, es, {
            "mm": ([128, 512], F32, 3),
            "tr": ([128, 128], F32, 1),
            "den": ([1, 512], F32, 1),
            "ctx": ([64, 512], F32, 1),
            "bc": ([64, 512], F32, 1),
            "lg": ([128, 8], F32, 1),
        })
        tmps = {"xt": xt, "xn": xn, "xr": xr, "xn2": xn2, "sq": sq, "r4": r4,
                "strideT": S}
        t_if = b.dma("sync", sif[:], identf[:, :], group="ident")
        t_o1 = b.emit("vector", lambda e: e.memset(o1r[0:1, :], 1.0))
        t_o1c = b.emit("vector", lambda e: e.memset(o1c[:, 0:1], 1.0))
        t_o64 = b.emit("vector", lambda e: e.memset(o64[0:1, :], 1.0))
        t_bq = _bias_cols(b, bq, bqc, 8, "bq")
        t_bk = _bias_cols(b, bk, bkc, 8, "bk")
        t_bvl = b.dma("sync", trow[0:1, 0:D], bv[:].rearrange("(one d) -> one d", one=1), group="bv")
        t_bvc = b.emit("scalar", lambda e: e.copy(out=brow[0:1, :], in_=trow[0:1, :]), deps=[t_bvl])
        t_bol = b.dma("sync", trow[0:1, 0:D], bo[:].rearrange("(one d) -> one d", one=1),
                      deps=[t_bvc], group="bv")
        t_boc = b.emit("scalar", lambda e: e.copy(out=brow2[0:1, :], in_=trow[0:1, :]), deps=[t_bol])
        t_gw = []
        war = None
        for j in range(8):
            tl = b.dma("sync", tg[:, 0:8], gwT[j * 128:(j + 1) * 128, :], deps=[war], group="gw")
            tc = b.emit("scalar", lambda e, j=j: e.copy(out=gwb[:, j * 8:(j + 1) * 8],
                                                        in_=tg[:, 0:8]), deps=[tl])
            war = tc
            t_gw.append(tc)

        for bb in range(NB):
            base = bb * S
            phA = []
            bar0 = b.bar()
            for st in range(16):
                phA += _lnT_from_x(b, x, base + st * 128, AR1, st * 128, tmps, sif[:], t_if, ps,
                                   extra_dep=bar0)
            phB = []
            for (wt, dst, mode, bias_c) in ((wqT, AR2, "f", bqc), (wkT, AR3, "f", bkc),
                                            (wvT, AR4, "t", None)):
                t_w = _load_wT_bf(b, wt, AW, D, D, tw, "w", deps=(phB[-1:] or []) + [b.tok("tensor")])
                if mode == "f":
                    for m in range(8):
                        for n in range(4):
                            pm, war = ps.get("mm")
                            tk = None
                            for k in range(8):
                                tk = b.emit("tensor", lambda e, pm=pm, m=m, n=n, k=k: e.matmul(
                                    pm[:, 0:512],
                                    AW[:, k * D + m * 128:k * D + (m + 1) * 128],
                                    AR1[:, k * S + n * 512:k * S + (n + 1) * 512],
                                    start=(k == 0), stop=(k == 7)),
                                    deps=(phA + t_w + [war]) if k == 0 else [], inc=(k == 7))
                            t_e = b.emit("vector", lambda e, pm=pm, m=m, n=n, dst=dst, bias_c=bias_c: e.tensor_scalar(
                                out=dst[:, m * S + n * 512:m * S + (n + 1) * 512],
                                in0=pm[:, 0:512], scalar1=bias_c[:, m:m + 1], scalar2=None,
                                op0=ALU.add), deps=[tk] + t_bq + t_bk)
                            ps.done("mm", t_e)
                            phB.append(t_e)
                else:
                    for sm in range(16):
                        for n in range(2):
                            pm, war = ps.get("mm")
                            tk = None
                            for k in range(8):
                                tk = b.emit("tensor", lambda e, pm=pm, sm=sm, n=n, k=k: e.matmul(
                                    pm[:, 0:512],
                                    AR1[:, k * S + sm * 128:k * S + (sm + 1) * 128],
                                    AW[:, k * D + n * 512:k * D + (n + 1) * 512],
                                    start=(k == 0), stop=False),
                                    deps=(phA + t_w + [war]) if k == 0 else [], inc=False)
                            tk = b.emit("tensor", lambda e, pm=pm, n=n: e.matmul(
                                pm[:, 0:512], o1r[0:1, 0:128], brow[0:1, n * 512:(n + 1) * 512],
                                start=False, stop=True), deps=[t_o1, t_bvc])
                            t_e = b.emit("scalar", lambda e, pm=pm, sm=sm, n=n: e.copy(
                                out=AR4[:, sm * D + n * 512:sm * D + (n + 1) * 512],
                                in_=pm[:, 0:512]), deps=[tk])
                            ps.done("mm", t_e)
                            phB.append(t_e)
            phC = []
            barB3 = b.tok("tensor")
            for h in range(16):
                koff = (h % 2) * 64
                kch = h // 2
                for qb in range(4):
                    snap = b.tok("tensor")
                    t_exp = []
                    for skc in range(16):
                        pm, war = ps.get("mm")
                        tk = b.emit("tensor", lambda e, pm=pm, skc=skc, qb=qb, koff=koff, kch=kch: e.matmul(
                            pm[:, 0:512],
                            AR3[koff:koff + 64, kch * S + skc * 128:kch * S + (skc + 1) * 128],
                            AR2[koff:koff + 64, kch * S + qb * 512:kch * S + (qb + 1) * 512],
                            start=True, stop=True), deps=phB + [war])
                        te = b.emit("scalar", lambda e, pm=pm, skc=skc: e.activation(
                            out=expT[:, skc * 512:(skc + 1) * 512], in_=pm[:, 0:512],
                            func=AF.Exp, scale=0.125), deps=[tk, snap])
                        ps.done("mm", te)
                        t_exp.append(te)
                    pd, war_d = ps.get("den")
                    tk = None
                    for skc in range(16):
                        tk = b.emit("tensor", lambda e, pd=pd, skc=skc: e.matmul(
                            pd[0:1, 0:512], o1c[0:128, 0:1], expT[:, skc * 512:(skc + 1) * 512],
                            start=(skc == 0), stop=(skc == 15)),
                            deps=(t_exp + [t_o1c, war_d]) if skc == 0 else [], inc=(skc == 15))
                    t_rr = b.emit("vector", lambda e, pd=pd: e.reciprocal(
                        out=rrow[0:1, 0:512], in_=pd[0:1, 0:512]), deps=[tk, snap])
                    ps.done("den", t_rr)
                    pb_, war_b = ps.get("bc")
                    t_bm = b.emit("tensor", lambda e, pb_=pb_: e.matmul(
                        pb_[0:64, 0:512], o64[0:1, 0:64], rrow[0:1, 0:512],
                        start=True, stop=True), deps=[t_rr, t_o64, war_b])
                    t_bc = b.emit("vector", lambda e, pb_=pb_: e.tensor_copy(
                        out=rb[0:64, :], in_=pb_[0:64, 0:512]), deps=[t_bm])
                    ps.done("bc", t_bc)
                    pc, war_c = ps.get("ctx")
                    tk = None
                    for skc in range(16):
                        tk = b.emit("tensor", lambda e, pc=pc, skc=skc, h=h: e.matmul(
                            pc[0:64, 0:512],
                            AR4[:, skc * D + h * 64:skc * D + h * 64 + 64],
                            expT[:, skc * 512:(skc + 1) * 512],
                            start=(skc == 0), stop=(skc == 15)),
                            deps=(t_exp + [war_c]) if skc == 0 else [], inc=(skc == 15))
                    t_ev = b.emit("vector", lambda e, pc=pc, h=h, qb=qb, koff=koff, kch=kch: e.tensor_tensor(
                        out=AR1[koff:koff + 64, kch * S + qb * 512:kch * S + (qb + 1) * 512],
                        in0=pc[0:64, 0:512], in1=rb[0:64, :], op=ALU.mult),
                        deps=[tk, t_bc, barB3])
                    ps.done("ctx", t_ev)
                    phC.append(t_ev)
            t_wo = _load_wT_bf(b, woT, AW, D, D, tw, "w", deps=phB + [b.tok("tensor")])
            phE = []
            for st in range(16):
                pms, mmt = [], []
                for n in range(2):
                    pm, war = ps.get("mm")
                    tk = None
                    for k in range(8):
                        tk = b.emit("tensor", lambda e, pm=pm, st=st, n=n, k=k: e.matmul(
                            pm[:, 0:512],
                            AR1[:, k * S + st * 128:k * S + (st + 1) * 128],
                            AW[:, k * D + n * 512:k * D + (n + 1) * 512],
                            start=(k == 0), stop=False),
                            deps=(phC + t_wo + [war]) if k == 0 else [], inc=False)
                    tk = b.emit("tensor", lambda e, pm=pm, n=n: e.matmul(
                        pm[:, 0:512], o1r[0:1, 0:128], brow2[0:1, n * 512:(n + 1) * 512],
                        start=False, stop=True), deps=[t_o1, t_boc])
                    pms.append(pm)
                    mmt.append(tk)
                t_o, t_a = _post_ln_residual(b, pms[0], pms[1], mmt, x, x3, base + st * 128, tmps)
                ps.done("mm", t_a)
                ps.done("mm", t_a)
                phE.append(t_o)
                phE.append(t_a)
            for st in range(16):
                row0 = base + st * 128
                t_l = b.dma("sync", xt[:, 0:D], x3[row0:row0 + 128, :],
                            deps=[tmps.get("war_xt")] + phE, group="xt")
                tmps["war_ln_out"] = tmps.get("war_xn")
                toks_ln = _ln_psum(b, [(xt[:, 0:512], 512), (xt[:, 512:1024], 512)], tmps, xn, [t_l])
                tmps["war_ln_out"] = None
                war_l3 = tmps.get("war_ln3s")
                chunk_toks = []
                t_tr = None
                for j in range(8):
                    pt, war = ps.get("tr")
                    t_tr = b.emit("tensor", lambda e, j=j, pt=pt: e.transpose(
                        pt[:, 0:128], xn[:, j * 128:(j + 1) * 128], sif[:]),
                        deps=toks_ln + [t_if, war])
                    t_cp = b.emit("scalar", lambda e, j=j, pt=pt: e.copy(
                        out=ln3s[:, j * 128:(j + 1) * 128], in_=pt[:, 0:128]),
                        deps=[t_tr, war_l3])
                    ps.done("tr", t_cp)
                    chunk_toks.append(t_cp)
                tmps["war_xt"] = toks_ln
                tmps["war_xn"] = t_tr
                t_sts = []
                for j in range(8):
                    t_sts.append(b.dma("gpsimd", ln3T[j * 128:(j + 1) * 128, row0:row0 + 128],
                                       ln3s[:, j * 128:(j + 1) * 128], deps=chunk_toks,
                                       group="l3o"))
                t_st = t_sts[-1]
                tmps["war_ln3s"] = t_sts
                pl, war_lg = ps.get("lg")
                tk = None
                for j in range(8):
                    tk = b.emit("tensor", lambda e, pl=pl, j=j: e.matmul(
                        pl[:, 0:8], ln3s[:, j * 128:(j + 1) * 128], gwb[:, j * 8:(j + 1) * 8],
                        start=(j == 0), stop=(j == 7)),
                        deps=(chunk_toks + t_gw + [war_lg]) if j == 0 else [], inc=(j == 7))
                t_lc = b.emit("vector", lambda e, pl=pl, st=st, bb=bb: e.tensor_copy(
                    out=lgt[:, (bb * 16 + st) * 8:(bb * 16 + st + 1) * 8], in_=pl[:, 0:8]),
                    deps=[tk])
                ps.done("lg", t_lc)
                if bb == NB - 1 and st == 15:
                    t_lo = b.dma("gpsimd", logits[:, :].rearrange("(c p) e -> p c e", p=128),
                                 lgt[:, :].rearrange("p (c e) -> p c e", e=8), deps=[t_lc] +
                                 [], group="lgo")
        b.finalize()
    return nc


def build_p4a():
    nc = bass.Bass("TRN2", target_bir_lowering=False)
    alll = nc.dram_tensor("alll", [128, 2048], F32, kind="ExternalInput").ap()
    own = nc.dram_tensor("own", [NT, 8], F32, kind="ExternalInput").ap()
    ones128 = nc.dram_tensor("ones128", [128, 128], F32, kind="ExternalInput").ap()
    gates = nc.dram_tensor("gates", [NT, 8], F32, kind="ExternalOutput").ap()
    K = 4 * 1024 * 6

    with contextlib.ExitStack() as es:
        E = es.enter_context
        b = B(nc, es)
        sal = E(nc.sbuf_tensor("sal", [128, 2048], F32))
        sol = E(nc.sbuf_tensor("sol", [128, 32 * 8], F32))
        son = E(nc.sbuf_tensor("son", [128, 128], F32))
        junk = E(nc.sbuf_tensor("junk", [128, 2048], F32))
        st_ = E(nc.sbuf_tensor("st_", [128, 8], F32))
        ps = PsumPool(nc, es, {"t": ([128, 1], F32, 2)})
        t_a = b.dma("sync", sal[:], alll[:, :], group="al")
        t_o = b.dma("sync", son[:], ones128[:, :], group="on")
        t_w = b.dma("sync", sol[:, :].rearrange("p (c e) -> p c e", e=8),
                    own[:, :].rearrange("(c p) e -> p c e", p=128), group="ow")
        t_lo = b.emit("vector", lambda e: e.memset(st_[:, 0:1], -1e4))
        t_hi = b.emit("vector", lambda e: e.memset(st_[:, 1:2], 1e4))
        last = [t_lo, t_hi, t_a, t_o]
        for it in range(60):
            t1 = b.emit("vector", lambda e: e.tensor_tensor(
                out=st_[:, 2:3], in0=st_[:, 0:1], in1=st_[:, 1:2], op=ALU.add), deps=last)
            t2 = b.emit("vector", lambda e: e.tensor_scalar(
                out=st_[:, 2:3], in0=st_[:, 2:3], scalar1=0.5, scalar2=None, op0=ALU.mult),
                deps=[t1])
            t3 = b.emit("vector", lambda e: e.tensor_scalar(
                out=junk[:], in0=sal[:], scalar1=st_[:, 2:3], scalar2=None,
                op0=ALU.is_le, op1=ALU.add, accum_out=st_[:, 3:4]), deps=[t2, t_a])
            pt, war = ps.get("t")
            t4 = b.emit("tensor", lambda e, pt=pt: e.matmul(
                pt[:, 0:1], son[:, :], st_[:, 3:4], start=True, stop=True),
                deps=[t3, t_o, war])
            t5 = b.emit("vector", lambda e, pt=pt: e.tensor_scalar(
                out=st_[:, 4:5], in0=pt[:, 0:1], scalar1=float(K), scalar2=None,
                op0=ALU.is_ge), deps=[t4])
            ps.done("t", t5)
            t6 = b.emit("vector", lambda e: e.tensor_tensor(
                out=st_[:, 6:7], in0=st_[:, 2:3], in1=st_[:, 1:2], op=ALU.subtract), deps=[t5])
            t7 = b.emit("vector", lambda e: e.tensor_tensor(
                out=st_[:, 6:7], in0=st_[:, 6:7], in1=st_[:, 4:5], op=ALU.mult), deps=[t6])
            t8 = b.emit("vector", lambda e: e.tensor_tensor(
                out=st_[:, 1:2], in0=st_[:, 1:2], in1=st_[:, 6:7], op=ALU.add), deps=[t7])
            t9 = b.emit("vector", lambda e: e.tensor_scalar(
                out=st_[:, 5:6], in0=st_[:, 4:5], scalar1=1.0, scalar2=None,
                op0=ALU.subtract), deps=[t5])
            t10 = b.emit("vector", lambda e: e.tensor_tensor(
                out=st_[:, 6:7], in0=st_[:, 2:3], in1=st_[:, 0:1], op=ALU.subtract), deps=[t8, t9])
            t11 = b.emit("vector", lambda e: e.tensor_tensor(
                out=st_[:, 6:7], in0=st_[:, 6:7], in1=st_[:, 5:6], op=ALU.mult), deps=[t10])
            t12 = b.emit("vector", lambda e: e.tensor_tensor(
                out=st_[:, 0:1], in0=st_[:, 0:1], in1=st_[:, 6:7], op=ALU.subtract), deps=[t11])
            last = [t12, t8]
        t_m = b.emit("vector", lambda e: e.tensor_scalar(
            out=junk[:, 0:256], in0=sol[:, :], scalar1=st_[:, 1:2], scalar2=None,
            op0=ALU.is_ge), deps=last + [t_w])
        t_g = b.emit("vector", lambda e: e.tensor_tensor(
            out=sol[:, :], in0=sol[:, :], in1=junk[:, 0:256], op=ALU.mult), deps=[t_m])
        t_out = b.dma("gpsimd", gates[:, :].rearrange("(c p) e -> p c e", p=128),
                      sol[:, :].rearrange("p (c e) -> p c e", e=8), deps=[t_g], group="go")
        b.emit("gpsimd", lambda e: e.nop(), deps=[t_out], inc=False)
        b.finalize()
    return nc


def build_p4b():
    nc = bass.Bass("TRN2", target_bir_lowering=False)
    ln3T = nc.dram_tensor("ln3T", [D, NT], BF16, kind="ExternalInput").ap()
    w1T = nc.dram_tensor("w1T", [D, 4 * D], F32, kind="ExternalInput").ap()
    b1 = nc.dram_tensor("b1", [4 * D], F32, kind="ExternalInput").ap()
    w2T = nc.dram_tensor("w2T", [4 * D, D], F32, kind="ExternalInput").ap()
    grow = nc.dram_tensor("grow", [1, NT], F32, kind="ExternalInput").ap()
    acc_in = nc.dram_tensor("acc_in", [NT, D], F32, kind="ExternalInput").ap()
    acc_out = nc.dram_tensor("acc_out", [NT, D], F32, kind="ExternalOutput").ap()
    FF = 4 * D

    with contextlib.ExitStack() as es:
        E = es.enter_context
        b = B(nc, es)
        w1b = E(nc.sbuf_tensor("w1b", [128, 8 * FF], BF16))
        w2b = E(nc.sbuf_tensor("w2b", [128, 32 * D], BF16))
        hT = E(nc.sbuf_tensor("hT", [128, 32 * 512], BF16))
        lnT = E(nc.sbuf_tensor("lnT", [128, 8 * 512], BF16))
        tw = E(nc.sbuf_tensor("tw", [128, 2048], F32))
        b1c = E(nc.sbuf_tensor("b1c", [128, 32], F32))
        gr = E(nc.sbuf_tensor("gr", [1, 512], F32))
        gb = E(nc.sbuf_tensor("gb", [128, 512], F32))
        o1r = E(nc.sbuf_tensor("o1r", [1, 128], F32))
        tx = E(nc.sbuf_tensor("tx", [128, 512], F32))
        t2s = E(nc.sbuf_tensor("t2s", [128, 512], F32))
        t4s = E(nc.sbuf_tensor("t4s", [128, 512], F32))
        at = E(nc.sbuf_tensor("at", [128, D], F32))
        ot = E(nc.sbuf_tensor("ot", [128, D], F32))
        ps = PsumPool(nc, es, {
            "mm": ([128, 512], F32, 4),
            "g": ([128, 512], F32, 2),
        })
        t_w1 = _load_wT_bf(b, w1T, w1b, D, FF, tw, "w1")
        t_w2 = _load_wT_bf(b, w2T, w2b, FF, D, tw, "w2", deps=t_w1[-1:])
        t_b1 = _bias_cols(b, b1, b1c, 32, "b1")

        t_o1 = b.emit("vector", lambda e: e.memset(o1r[0:1, :], 1.0))
        war_ln = None
        war_acc = None
        tmpsg = {}
        prev_tf = prev_td = prev_te = prev_tg = None
        for sb in range(8):
            snap_t = b.tok("tensor")
            t_ls = []
            for j in range(8):
                t_ls.append(b.dma("sync", lnT[:, j * 512:(j + 1) * 512],
                                  ln3T[j * 128:(j + 1) * 128, sb * 512:(sb + 1) * 512],
                                  deps=[war_ln], group="ln"))
            t_g = b.dma("sync", gr[0:1, 0:512], grow[0:1, sb * 512:(sb + 1) * 512],
                        deps=[tmpsg.get("war_gr")], group="g")
            pg, war_g = ps.get("g")
            t_gm = b.emit("tensor", lambda e, pg=pg: e.matmul(
                pg[:, 0:512], o1r[0:1, 0:128], gr[0:1, 0:512],
                start=True, stop=True), deps=[t_g, t_o1, war_g])
            tmpsg["war_gr"] = t_gm
            t_gc = b.emit("vector", lambda e, pg=pg: e.tensor_copy(
                out=gb[:, 0:512], in_=pg[:, 0:512]), deps=[t_gm, war_ln])
            ps.done("g", t_gc)
            phH = []
            for fc in range(32):
                pm, war = ps.get("mm")
                tk = None
                for k in range(8):
                    tk = b.emit("tensor", lambda e, pm=pm, fc=fc, k=k: e.matmul(
                        pm[:, 0:512],
                        w1b[:, k * FF + fc * 128:k * FF + (fc + 1) * 128],
                        lnT[:, k * 512:(k + 1) * 512],
                        start=(k == 0), stop=(k == 7)),
                        deps=(t_ls + t_w1 + [war]) if k == 0 else [], inc=(k == 7))
                ta = b.emit("vector", lambda e, pm=pm, fc=fc: e.tensor_scalar(
                    out=tx[:, 0:512], in0=pm[:, 0:512], scalar1=b1c[:, fc:fc + 1], scalar2=None,
                    op0=ALU.add), deps=[tk, prev_tg] + t_b1)
                tb = b.emit("scalar", lambda e, pm=pm, fc=fc: e.activation(
                    out=t2s[:, 0:512], in_=pm[:, 0:512], func=AF.Square,
                    bias=b1c[:, fc:fc + 1], scale=1.0), deps=[tk, ta, prev_td])
                tc = b.emit("vector", lambda e: e.tensor_scalar(
                    out=t2s[:, 0:512], in0=t2s[:, 0:512], scalar1=0.0713548162, scalar2=1.5957691,
                    op0=ALU.mult, op1=ALU.add), deps=[tb])
                td = b.emit("vector", lambda e: e.tensor_tensor(
                    out=t4s[:, 0:512], in0=t2s[:, 0:512], in1=tx[:, 0:512], op=ALU.mult),
                    deps=[tc, ta, prev_te])
                te = b.emit("scalar", lambda e: e.activation(
                    out=t4s[:, 0:512], in_=t4s[:, 0:512], func=AF.Sigmoid), deps=[td])
                tf = b.emit("vector", lambda e: e.tensor_tensor(
                    out=tx[:, 0:512], in0=tx[:, 0:512], in1=t4s[:, 0:512], op=ALU.mult),
                    deps=[te])
                tg_ = b.emit("vector", lambda e, fc=fc: e.tensor_tensor(
                    out=hT[:, fc * 512:(fc + 1) * 512], in0=tx[:, 0:512], in1=gb[:, 0:512],
                    op=ALU.mult), deps=[tf, t_gc, snap_t])
                prev_tf, prev_td, prev_te, prev_tg = tf, td, te, tg_
                ps.done("mm", tg_)
                phH.append(tg_)
            war_ln = b.tok("tensor")
            for ss in range(4):
                row0 = sb * 512 + ss * 128
                t_ai = b.dma("sync", at[:, 0:D], acc_in[row0:row0 + 128, :],
                             deps=[war_acc], group="ai")
                pms, mmt = [], []
                for n in range(2):
                    pm, war = ps.get("mm")
                    tk = None
                    for fc in range(32):
                        tk = b.emit("tensor", lambda e, pm=pm, ss=ss, n=n, fc=fc: e.matmul(
                            pm[:, 0:512],
                            hT[:, fc * 512 + ss * 128:fc * 512 + (ss + 1) * 128],
                            w2b[:, fc * D + n * 512:fc * D + (n + 1) * 512],
                            start=(fc == 0), stop=(fc == 31)),
                            deps=(phH + t_w2 + [war]) if fc == 0 else [], inc=(fc == 31))
                    pms.append(pm)
                    mmt.append(tk)
                ta = b.emit("vector", lambda e, pms=pms: e.tensor_tensor(
                    out=ot[:, 0:512], in0=pms[0][:, 0:512], in1=at[:, 0:512], op=ALU.add),
                    deps=mmt + [t_ai])
                tb = b.emit("vector", lambda e, pms=pms: e.tensor_tensor(
                    out=ot[:, 512:1024], in0=pms[1][:, 0:512], in1=at[:, 512:1024], op=ALU.add),
                    deps=mmt + [t_ai, ta])
                ps.done("mm", ta)
                ps.done("mm", tb)
                t_st = b.dma("gpsimd", acc_out[row0:row0 + 128, :], ot[:, 0:D],
                             deps=[ta, tb], group="ao")
                war_acc = [tb, t_st]
        b.emit("gpsimd", lambda e: e.nop(), deps=[t_st], inc=False)
        b.finalize()
    return nc


def build_p4c():
    nc = bass.Bass("TRN2", target_bir_lowering=False)
    acc = nc.dram_tensor("acc", [NT, D], F32, kind="ExternalInput").ap()
    gatesT = nc.dram_tensor("gatesT", [8, NT], F32, kind="ExternalInput").ap()
    b2all = nc.dram_tensor("b2all", [8, D], F32, kind="ExternalInput").ap()
    x3 = nc.dram_tensor("x3", [NT, D], F32, kind="ExternalInput").ap()
    out = nc.dram_tensor("out", [NT, D], F32, kind="ExternalOutput").ap()

    with contextlib.ExitStack() as es:
        E = es.enter_context
        b = B(nc, es)
        sg = E(nc.sbuf_tensor("sg", [8, NT], F32))
        sgb = E(nc.sbuf_tensor("sgb", [8, NT], BF16))
        sb2 = E(nc.sbuf_tensor("sb2", [8, D], F32))
        sb2b = E(nc.sbuf_tensor("sb2b", [8, D], BF16))
        at = E(nc.sbuf_tensor("at", [128, D], F32))
        xr = E(nc.sbuf_tensor("xr", [128, D], F32))
        xn2 = E(nc.sbuf_tensor("xn2", [128, D], F32))
        ffo = E(nc.sbuf_tensor("ffo", [128, D], F32))
        sq = E(nc.sbuf_tensor("sq", [128, 8], F32))
        r4 = E(nc.sbuf_tensor("r4", [128, 4], F32))
        ps = PsumPool(nc, es, {"mm": ([128, 512], F32, 4)})
        tmps = {"xr": xr, "xn2": xn2, "sq": sq, "r4": r4}
        t_g = b.dma("sync", sg[0:8, :], gatesT[:, :], group="g")
        t_gc = b.emit("scalar", lambda e: e.copy(out=sgb[0:8, :], in_=sg[0:8, :]), deps=[t_g])
        t_b = b.dma("sync", sb2[0:8, :], b2all[:, :], group="b")
        t_bc = b.emit("scalar", lambda e: e.copy(out=sb2b[0:8, :], in_=sb2[0:8, :]), deps=[t_b])
        war_at = None
        for st in range(32):
            row0 = st * 128
            t_ai = b.dma("sync", at[:, 0:D], acc[row0:row0 + 128, :], deps=[war_at], group="ai")
            pms, mmt = [], []
            for n in range(2):
                pm, war = ps.get("mm")
                tk = b.emit("tensor", lambda e, pm=pm, st=st, n=n: e.matmul(
                    pm[:, 0:512], sgb[0:8, st * 128:(st + 1) * 128],
                    sb2b[0:8, n * 512:(n + 1) * 512], start=True, stop=True),
                    deps=[t_gc, t_bc, war])
                pms.append(pm)
                mmt.append(tk)
            ta = b.emit("vector", lambda e, pms=pms: e.tensor_tensor(
                out=ffo[:, 0:512], in0=pms[0][:, 0:512], in1=at[:, 0:512], op=ALU.add),
                deps=mmt + [t_ai])
            tb = b.emit("vector", lambda e, pms=pms: e.tensor_tensor(
                out=ffo[:, 512:1024], in0=pms[1][:, 0:512], in1=at[:, 512:1024], op=ALU.add),
                deps=mmt + [t_ai, ta])
            ps.done("mm", ta)
            ps.done("mm", tb)
            war_at = tb
            t_x = b.dma("sync", xr[:, 0:D], x3[row0:row0 + 128, :],
                        deps=[tmps.get("war_xr")], group="xr")
            tmps["war_ln_out"] = tmps.get("war_xn2")
            toks = _ln_psum(b, [(ffo[:, 0:512], 512), (ffo[:, 512:1024], 512)], tmps, xn2,
                            [ta, tb])
            tmps["war_ln_out"] = None
            t_a2 = b.emit("vector", lambda e: e.tensor_tensor(
                out=xn2[:, 0:D], in0=xn2[:, 0:D], in1=xr[:, 0:D], op=ALU.add),
                deps=toks + [t_x])
            tmps["war_xr"] = t_a2
            t_o = b.dma("gpsimd", out[row0:row0 + 128, :], xn2[:, 0:D], deps=[t_a2], group="o")
            tmps["war_xn2"] = t_o
        b.emit("gpsimd", lambda e: e.nop(), deps=[t_o], inc=False)
        b.finalize()
    return nc


P1_STOP = "Z"
_programs = {}


def _get(name, fn):
    if name not in _programs:
        _programs[name] = fn()
    return _programs[name]


def kernel(**inp):
    f32 = np.float32
    ncore = 8
    x0 = np.ascontiguousarray(inp["L0_attn_hidden_states"].astype(f32)).reshape(16, 2048, 1024)
    identf = np.eye(128, dtype=f32)
    identb = np.eye(128).astype(ml_dtypes.bfloat16)
    ones128 = np.ones((128, 128), dtype=f32)

    def T(a):
        return np.ascontiguousarray(np.asarray(a, dtype=f32).T)

    shards = [np.ascontiguousarray(x0[2 * c:2 * c + 2].reshape(NT, D)) for c in range(ncore)]
    cores = list(range(ncore))

    p1 = _get("p1", build_p1)
    common1 = dict(wqT=T(inp["L0_attn_Wq"]), wkT=T(inp["L0_attn_Wk"]),
                   wvT=T(inp["L0_attn_Wv"]), woT=T(inp["L0_attn_Wo"]),
                   wfT=np.vstack([T(inp["L0_attn_Wf"]), T(inp["L0_attn_Wf"])]), identf=identf, identb=identb)
    res = run_bass_kernel_spmd(p1, [dict(x=s, **common1) for s in shards], core_ids=cores)
    x1 = [r["x1"] for r in res.results]

    p2 = _get("p2", build_p2)
    common2 = dict(w1T=T(inp["L0_ffn_W1"]), b1=np.asarray(inp["L0_ffn_b1"], f32),
                   w2T=T(inp["L0_ffn_W2"]), b2=np.asarray(inp["L0_ffn_b2"], f32),
                   identf=identf)
    res = run_bass_kernel_spmd(p2, [dict(x=s, **common2) for s in x1], core_ids=cores)
    x2 = [r["x2"] for r in res.results]

    p3 = _get("p3", build_p3)
    common3 = dict(wqT=T(inp["L1_attn_Wq"]), wkT=T(inp["L1_attn_Wk"]),
                   wvT=T(inp["L1_attn_Wv"]), woT=T(inp["L1_attn_Wo"]),
                   bq=np.asarray(inp["L1_attn_bq"], f32), bk=np.asarray(inp["L1_attn_bk"], f32),
                   bv=np.asarray(inp["L1_attn_bv"], f32), bo=np.asarray(inp["L1_attn_bo"], f32),
                   gwT=T(inp["L1_ffn_gate_weight"]), identf=identf)
    res = run_bass_kernel_spmd(p3, [dict(x=s, **common3) for s in x2], core_ids=cores)
    x3 = [r["x3"] for r in res.results]
    ln3T = [r["ln3T"] for r in res.results]
    logits = [r["logits"] for r in res.results]

    alll = np.ascontiguousarray(np.concatenate(logits, axis=0).reshape(128, 2048))
    p4a = _get("p4a", build_p4a)
    res = run_bass_kernel_spmd(
        p4a, [dict(alll=alll, own=lg, ones128=ones128) for lg in logits], core_ids=cores)
    gates = [r["gates"] for r in res.results]

    p4b = _get("p4b", build_p4b)
    w1 = np.asarray(inp["L1_ffn_w1"], f32)
    w2 = np.asarray(inp["L1_ffn_w2"], f32)
    b1e = np.asarray(inp["L1_ffn_b1"], f32)
    acc = [np.zeros((NT, D), dtype=f32) for _ in cores]
    for e in range(8):
        w1Te = np.ascontiguousarray(w1[e].T)
        w2Te = np.ascontiguousarray(w2[e].T)
        in_maps = [dict(ln3T=ln3T[c], w1T=w1Te, b1=b1e[e], w2T=w2Te,
                        grow=np.ascontiguousarray(gates[c][:, e].reshape(1, NT)),
                        acc_in=acc[c]) for c in cores]
        res = run_bass_kernel_spmd(p4b, in_maps, core_ids=cores)
        acc = [r["acc_out"] for r in res.results]

    p4c = _get("p4c", build_p4c)
    b2all = np.asarray(inp["L1_ffn_b2"], f32)
    in_maps = [dict(acc=acc[c], gatesT=np.ascontiguousarray(gates[c].T),
                    b2all=b2all, x3=x3[c]) for c in cores]
    res = run_bass_kernel_spmd(p4c, in_maps, core_ids=cores)
    outs = [r["out"] for r in res.results]
    return np.concatenate([o.reshape(2, 2048, 1024) for o in outs], axis=0)
